# version 02
# baseline (speedup 1.0000x reference)
import numpy as np
import concourse.bass as bass
import concourse.mybir as mybir
from concourse.bass_utils import run_bass_kernel_spmd

F32 = mybir.dt.float32
BF16 = mybir.dt.bfloat16
AF = mybir.ActivationFunctionType
ALU = mybir.AluOpType
AX = mybir.AxisListType

D = 2048
T = 1024
NH = 16
HD = 128
FC = D // 128
NEG = -1e30
SCALE0 = HD ** -0.5
DEBUG_ATTN = False
NBLK = 24
SKIP_TRI = False
FIXED_C = None


class Ctx:

    ENGINES = ("tensor", "vector", "scalar", "gpsimd", "sync")
    ESEM = {"vector": "v", "scalar": "s", "gpsimd": "g", "tensor": "t"}

    def __init__(self, nc):
        self.nc = nc
        self.steps = {e: [] for e in self.ENGINES}
        self.cnt = {}
        self.sems = {}

    def sem(self, name):
        if name not in self.cnt:
            self.cnt[name] = 0
        return name

    def emit(self, engine, fn, waits=(), inc=None, inc_amt=1):
        post = None
        if inc is not None:
            self.sem(inc)
            self.cnt[inc] += inc_amt
            post = (inc, inc_amt)
        ws = [(s, v) for (s, v) in waits if v > 0]
        self.steps[engine].append((fn, ws, post))
        return self.cnt[inc] if inc else None

    def run(self, engine, fn, waits=()):
        s = self.ESEM[engine]
        v = self.emit(engine, fn, waits=waits, inc=s)
        return (s, v)

    def c(self, name):
        return self.cnt.get(name, 0)

    def all_eng(self):
        return [(s, self.c(s)) for s in ("t", "v", "s", "g")]

    def replay(self, engine, eng):
        for fn, waits, post in self.steps[engine]:
            for (s, v) in waits:
                eng.wait_ge(self.sems[s], v)
            inst = fn(eng)
            if post is not None:
                s, amt = post
                if inst is None:
                    raise RuntimeError(f"step on {engine} wants inc {s}, no inst")
                inst.then_inc(self.sems[s], amt)

    def finish(self, block):
        @block.tensor
        def _(eng):
            self.replay("tensor", eng)

        @block.vector
        def _(eng):
            self.replay("vector", eng)

        @block.scalar
        def _(eng):
            self.replay("scalar", eng)

        @block.gpsimd
        def _(eng):
            self.replay("gpsimd", eng)

        @block.sync
        def _(eng):
            self.replay("sync", eng)


def open_sems(nc, ctx, names):
    import contextlib
    stack = contextlib.ExitStack()
    for n in names:
        ctx.sems[n] = stack.enter_context(nc.semaphore(n))
        ctx.sem(n)
    return stack


class NS:
    def __init__(self, **kw):
        self.__dict__.update(kw)


ALL_SEMS = ["ld", "wl", "t", "v", "s", "g", "st", "cc"]


def dma_in(ctx, queue, dst_ap, src_ap, waits=(), sem="ld"):
    return ctx.emit(queue, lambda e, dd=dst_ap, ss=src_ap: e.dma_start(out=dd, in_=ss),
                    waits=waits, inc=sem, inc_amt=16)


def dma_out(ctx, queue, dst_ap, src_ap, waits, sem="st"):
    return ctx.emit(queue, lambda e, dd=dst_ap, ss=src_ap: e.dma_start(out=dd, in_=ss),
                    waits=waits, inc=sem, inc_amt=16)


def emit_final_wait(ctx):
    ctx.emit("gpsimd", lambda e: None, waits=[("st", ctx.c("st"))])


def stream_cast(ctx, wf, nslots, marks, f, src_ap, dst_ap, extra_cast_waits=()):
    slot = f % nslots
    W = src_ap.shape[-1]
    dw = [marks[slot]] if slot in marks else []
    dma_in(ctx, "sync", wf[:, slot, 0:W], src_ap, waits=dw, sem="wl")
    wl = ctx.c("wl")
    eng = ("vector", "scalar", "gpsimd")[f % 3]
    s = ctx.ESEM[eng]
    if eng == "scalar":
        fn = lambda e, slot=slot, W=W, d=dst_ap: e.copy(d, wf[:, slot, 0:W])
    else:
        fn = lambda e, slot=slot, W=W, d=dst_ap: e.tensor_copy(d, wf[:, slot, 0:W])
    mk = ctx.run(eng, fn, waits=[("wl", wl), (s, ctx.c(s))] + list(extra_cast_waits))
    marks[slot] = mk
    return mk


def emit_layernorm(ctx, ns, x, xnbf, xw):
    ps, sq, stat = ns.ps, ns.sq, ns.stat
    for i in range(FC):
        sv = ctx.run("scalar", lambda e, i=i: e.activation(sq[:, :], x[:, i, :], AF.Square),
                     waits=xw + [("t", ctx.c("t")), ("s", ctx.c("s"))])
        for half in range(2):
            ctx.emit("tensor", lambda e, i=i, h=half: e.matmul(
                ps[0:1, h, :], ns.ones128[:, :], x[:, i, 512 * h:512 * (h + 1)],
                start=(i == 0), stop=(i == FC - 1)),
                waits=(xw + [("g", ns.g_const)] if (i == 0 and half == 0) else []))
        for half in range(2):
            ctx.emit("tensor", lambda e, i=i, h=half: e.matmul(
                ps[0:1, 2 + h, :], ns.ones128[:, :], sq[:, 512 * h:512 * (h + 1)],
                start=(i == 0), stop=(i == FC - 1)),
                waits=[sv], inc=("t" if half == 1 else None))
    t_stats = ctx.c("t")

    vw = [("t", t_stats), ("v", ctx.c("v"))]
    m = ctx.run("vector", lambda e: e.tensor_scalar_mul(
        stat[:, 0, :], ps[0:1, 0:2, :].rearrange("p a b -> p (a b)"), 1.0 / D), waits=vw)
    v1 = ctx.run("vector", lambda e: e.tensor_scalar_mul(
        stat[:, 1, :], ps[0:1, 2:4, :].rearrange("p a b -> p (a b)"), 1.0 / D), waits=[m])
    v2 = ctx.run("vector", lambda e: e.scalar_tensor_tensor(
        stat[:, 2, :], stat[:, 0, :], 1.0, stat[:, 0, :], ALU.mult, ALU.mult), waits=[v1])
    v3 = ctx.run("vector", lambda e: e.scalar_tensor_tensor(
        stat[:, 1, :], stat[:, 1, :], 1e-5, stat[:, 2, :], ALU.add, ALU.subtract),
        waits=[v2])
    s1 = ctx.run("scalar", lambda e: e.activation(stat[:, 2, :], stat[:, 1, :], AF.Sqrt),
                 waits=[v3, ("s", ctx.c("s"))])
    v4 = ctx.run("vector", lambda e: e.reciprocal(stat[:, 3, :], stat[:, 2, :]), waits=[s1])
    v5 = ctx.run("vector", lambda e: e.tensor_mul(stat[:, 4, :], stat[:, 0, :], stat[:, 3, :]),
                 waits=[v4])
    for half in range(2):
        ctx.emit("tensor", lambda e, h=half: e.matmul(
            ps[:, 4 + h, :], ns.ones1r[:, :], stat[:, 3, 512 * h:512 * (h + 1)],
            start=True, stop=True), waits=([v4, ("g", ns.g_const)] if half == 0 else []))
    for half in range(2):
        ctx.emit("tensor", lambda e, h=half: e.matmul(
            ps[:, 6 + h, :], ns.ones1r[:, :], stat[:, 4, 512 * h:512 * (h + 1)],
            start=True, stop=True), waits=[v5], inc=("t" if half == 1 else None))
    t_bc = ctx.c("t")

    last = None
    for i in range(FC):
        a = ctx.run("vector", lambda e, i=i: e.tensor_mul(
            sq[:, :], x[:, i, :], ps[:, 4:6, :].rearrange("p a b -> p (a b)")),
            waits=[("t", t_bc), ("v", ctx.c("v")), ("s", ctx.c("s"))])
        if ns.xnlo is None:
            last = ctx.run("vector", lambda e, i=i: e.tensor_sub(
                xnbf[:, i, :], sq[:, :], ps[:, 6:8, :].rearrange("p a b -> p (a b)")),
                waits=[a])
        else:
            b1 = ctx.run("vector", lambda e, i=i: e.tensor_sub(
                ns.sqf[:, :], sq[:, :], ps[:, 6:8, :].rearrange("p a b -> p (a b)")),
                waits=[a])
            b2 = ctx.run("vector", lambda e, i=i: e.tensor_copy(
                xnbf[:, i, :], ns.sqf[:, :]), waits=[b1])
            b3 = ctx.run("vector", lambda e, i=i: e.tensor_copy(
                sq[:, :], xnbf[:, i, :]), waits=[b2])
            last = ctx.run("vector", lambda e, i=i: e.tensor_sub(
                ns.xnlo[:, i, :], ns.sqf[:, :], sq[:, :]), waits=[b3])
    return last



def build_l0pre():
    nc = bass.Bass("TRN2", target_bir_lowering=False)
    d = NS()
    d.xT = nc.dram_tensor("xT", [D, T], F32, kind="ExternalInput").ap()
    d.wqT = nc.dram_tensor("wqT", [D, D], F32, kind="ExternalInput").ap()
    d.wkT = nc.dram_tensor("wkT", [D, D], F32, kind="ExternalInput").ap()
    d.wvT = nc.dram_tensor("wvT", [D, D], F32, kind="ExternalInput").ap()
    d.ball = nc.dram_tensor("ball_i", [128, 2, FC], F32, kind="ExternalInput").ap()
    d.bvrow = nc.dram_tensor("bvrow_i", [1, D], F32, kind="ExternalInput").ap()
    d.cosq = nc.dram_tensor("cosq_i", [128, T], F32, kind="ExternalInput").ap()
    d.sinq = nc.dram_tensor("sinq_i", [128, T], F32, kind="ExternalInput").ap()
    d.xnbf_o = nc.dram_tensor("xnbf_o", [D, T], BF16, kind="ExternalOutput").ap()
    d.xnlo_o = nc.dram_tensor("xnlo_o", [D, T], BF16, kind="ExternalOutput").ap()
    d.qr_o = nc.dram_tensor("qr_o", [D, T], BF16, kind="ExternalOutput").ap()
    d.kr_o = nc.dram_tensor("kr_o", [D, T], BF16, kind="ExternalOutput").ap()
    d.vtm_o = nc.dram_tensor("vtm_o", [T, D], BF16, kind="ExternalOutput").ap()

    ctx = Ctx(nc)
    import contextlib
    with contextlib.ExitStack() as stack:
        xnbf = stack.enter_context(nc.sbuf_tensor("xnbf", [128, FC, T], BF16))
        xnlo = stack.enter_context(nc.sbuf_tensor("xnlo", [128, FC, T], BF16))
        wf = stack.enter_context(nc.sbuf_tensor("wf", [128, 1, D], F32))
        sq = stack.enter_context(nc.sbuf_tensor("sq", [128, T], F32))
        sqf = stack.enter_context(nc.sbuf_tensor("sqf", [128, T], F32))
        stat = stack.enter_context(nc.sbuf_tensor("stat", [1, 5, T], F32))
        pre = stack.enter_context(nc.sbuf_tensor("pre", [128, 2, T], F32))
        rt1 = stack.enter_context(nc.sbuf_tensor("rt1", [64, 1, T], F32))
        rt2 = stack.enter_context(nc.sbuf_tensor("rt2", [64, 1, T], F32))
        qk = stack.enter_context(nc.sbuf_tensor("qk", [128, 2, T], BF16))
        vtm = stack.enter_context(nc.sbuf_tensor("vtm", [128, 1, D], BF16))
        cs = stack.enter_context(nc.sbuf_tensor("cs", [128, 2, T], F32))
        ball = stack.enter_context(nc.sbuf_tensor("ball", [128, 2, FC], F32))
        bvrow = stack.enter_context(nc.sbuf_tensor("bvrow", [1, D], F32))
        ones128 = stack.enter_context(nc.sbuf_tensor("ones128", [128, 1], F32))
        ones1r = stack.enter_context(nc.sbuf_tensor("ones1r", [1, 128], F32))
        ones1rb = stack.enter_context(nc.sbuf_tensor("ones1rb", [1, 128], BF16))
        ps = stack.enter_context(nc.psum_tensor("ps", [128, 8, 512], F32))
        block = stack.enter_context(nc.Block())
        x_guard = nc.sbuf_tensor("x", [128, FC, T], F32)
        x = x_guard.__enter__()

        ns = NS(ps=ps, sq=sq, stat=stat, ones128=ones128, ones1r=ones1r,
                xnlo=xnlo, sqf=sqf)
        stack.enter_context(open_sems(nc, ctx, ALL_SEMS))

        ctx.run("gpsimd", lambda e: e.memset(ones128[:, :], 1.0))
        ctx.run("gpsimd", lambda e: e.memset(ones1r[:, :], 1.0))
        ns.g_const = ctx.c("g")

        for i in range(4):
            dma_in(ctx, "sync", x[:, 4 * i:4 * (i + 1), :],
                   d.xT.rearrange("(c p) t -> p c t", p=128)[:, 4 * i:4 * (i + 1), :])
        dma_in(ctx, "sync", ball[:, :, :], d.ball)
        dma_in(ctx, "sync", bvrow[:, :], d.bvrow)
        dma_in(ctx, "sync", cs[:, 0, :], d.cosq)
        dma_in(ctx, "sync", cs[:, 1, :], d.sinq)
        ld_all = ctx.c("ld")
        xw = [("ld", 64)]

        v_xn = emit_layernorm(ctx, ns, x, xnbf, xw)

        for i in range(FC):
            w = [v_xn] if i == FC - 1 else [("v", v_xn[1] - 5 * (FC - 1 - i))]
            dma_out(ctx, "gpsimd", d.xnbf_o.rearrange("(c p) t -> p c t", p=128)[:, i, :],
                    xnbf[:, i, :], waits=w)
            dma_out(ctx, "gpsimd", d.xnlo_o.rearrange("(c p) t -> p c t", p=128)[:, i, :],
                    xnlo[:, i, :], waits=w)

        x_guard.__exit__(None, None, None)
        wb_guard = nc.sbuf_tensor("wb", [128, FC, D], BF16)
        wb = wb_guard.__enter__()

        marks = {}
        pre_read = {}
        bank_free = {}
        qk_free = {}
        for which, wT, brow, out_hbm in ((0, d.wqT, 0, d.qr_o), (1, d.wkT, 1, d.kr_o)):
            t_before = ctx.c("t")
            for f in range(FC):
                stream_cast(ctx, wf, 1, marks, f,
                            wT.rearrange("(c p) o -> p c o", p=128)[:, f, :],
                            wb[:, f, :],
                            extra_cast_waits=[("t", t_before), v_xn])
            cast_done = ctx.all_eng()
            for j in range(NH):
                bank = (j % 2) * 2
                w0 = list(cast_done) if j < 2 else []
                if bank in bank_free:
                    w0 = w0 + [bank_free[bank]]
                for f in range(FC):
                    for half in range(2):
                        inc = "t" if (f == FC - 1 and half == 1) else None
                        ctx.emit("tensor", lambda e, f=f, h=half, j=j, bank=bank: e.matmul(
                            ps[:, bank + h, :], wb[:, f, 128 * j:128 * (j + 1)],
                            xnbf[:, f, 512 * h:512 * (h + 1)],
                            start=(f == 0), stop=(f == FC - 1)),
                            waits=(w0 if (f == 0 and half == 0) else []), inc=inc)
                tj = ctx.c("t")
                slot = j % 2
                sw = [("t", tj), ("s", ctx.c("s")), ("ld", ld_all)]
                if slot in pre_read:
                    sw.append(pre_read[slot])
                sb = ctx.run("scalar", lambda e, j=j, slot=slot, bank=bank, brow=brow:
                             e.add(pre[:, slot, :],
                                   ps[:, bank:bank + 2, :].rearrange("p a b -> p (a b)"),
                                   ball[:, brow, j:j + 1]),
                             waits=sw)
                bank_free[bank] = sb
                wq0 = [sb, ("v", ctx.c("v"))]
                r1 = ctx.run("vector", lambda e, slot=slot: e.tensor_mul(
                    rt1[:, 0, :], pre[0:64, slot, :], cs[0:64, 0, :]), waits=wq0)
                r2 = ctx.run("vector", lambda e, slot=slot: e.tensor_mul(
                    rt2[:, 0, :], pre[64:128, slot, :], cs[64:128, 1, :]), waits=[r1])
                w3 = [r2] + ([qk_free[slot]] if slot in qk_free else [])
                r3 = ctx.run("vector", lambda e, slot=slot: e.tensor_sub(
                    qk[0:64, slot, :], rt1[:, 0, :], rt2[:, 0, :]), waits=w3)
                r4 = ctx.run("vector", lambda e, slot=slot: e.tensor_mul(
                    rt1[:, 0, :], pre[0:64, slot, :], cs[0:64, 1, :]), waits=[r3])
                r5 = ctx.run("vector", lambda e, slot=slot: e.tensor_mul(
                    rt2[:, 0, :], pre[64:128, slot, :], cs[64:128, 0, :]), waits=[r4])
                r6 = ctx.run("vector", lambda e, slot=slot: e.tensor_add(
                    qk[64:128, slot, :], rt1[:, 0, :], rt2[:, 0, :]), waits=[r5])
                pre_read[slot] = r6
                dma_out(ctx, "gpsimd",
                        out_hbm.rearrange("(c p) t -> p c t", p=128)[:, j, :],
                        qk[:, slot, :], waits=[r6])
                qk_free[slot] = ("st", ctx.c("st"))

        t_before = ctx.c("t")
        for f in range(FC):
            stream_cast(ctx, wf, 1, marks, f,
                        d.wvT.rearrange("(c p) o -> p c o", p=128)[:, f, :],
                        wb[:, f, :], extra_cast_waits=[("t", t_before)])
        vcast = ctx.all_eng() + [("ld", ld_all)]
        evac_mark = {}
        vtm_free = {}
        for tt in range(8):
            vslot = 0
            for og in range(4):
                g = tt * 4 + og
                bank = g % 2
                w0 = list(vcast) if g < 2 else []
                if bank in evac_mark:
                    w0 = w0 + [evac_mark[bank]]
                for f in range(FC):
                    ctx.emit("tensor", lambda e, f=f, tt=tt, og=og, bank=bank: e.matmul(
                        ps[:, bank, :], xnbf[:, f, 128 * tt:128 * (tt + 1)],
                        wb[:, f, 512 * og:512 * (og + 1)],
                        start=(f == 0), stop=False),
                        waits=(w0 if f == 0 else []))
                tv = ctx.emit("tensor", lambda e, og=og, bank=bank: e.matmul(
                    ps[:, bank, :], ones1r[0:1, 0:128], bvrow[:, 512 * og:512 * (og + 1)],
                    start=False, stop=True), waits=[], inc="t")
                ew = [("t", tv), ("s", ctx.c("s"))]
                if og == 0 and vslot in vtm_free:
                    ew.append(vtm_free[vslot])
                ev = ctx.run("scalar", lambda e, vslot=vslot, og=og, bank=bank: e.copy(
                    vtm[:, vslot, 512 * og:512 * (og + 1)], ps[:, bank, :]), waits=ew)
                evac_mark[bank] = ev
            dma_out(ctx, "gpsimd", d.vtm_o.rearrange("(c p) dd -> p c dd", p=128)[:, tt, :],
                    vtm[:, vslot, :], waits=[evac_mark[0], evac_mark[1]])
            vtm_free[vslot] = ("st", ctx.c("st"))

        emit_final_wait(ctx)
        ctx.finish(block)
        wb_guard.__exit__(None, None, None)
    return nc



def rope_perm():
    p = np.concatenate([np.arange(0, HD, 2), np.arange(1, HD, 2)])
    return np.concatenate([h * HD + p for h in range(NH)])


def rope_tables(pos):
    half = HD // 2
    freqs = 10000.0 ** (-np.arange(half, dtype=np.float64) / half)
    theta = freqs[:, None] * pos[None, :].astype(np.float64)
    return (np.cos(theta).astype(np.float32), np.sin(theta).astype(np.float32))


def prep_l0pre_inputs(L0_attn_x, L0_attn_w_qkv, L0_attn_b_qkv):
    perm = rope_perm()
    wq = L0_attn_w_qkv[0:D][perm]
    wk = L0_attn_w_qkv[D:2 * D][perm]
    wv = L0_attn_w_qkv[2 * D:3 * D]
    bq = L0_attn_b_qkv[0:D][perm]
    bk = L0_attn_b_qkv[D:2 * D][perm]
    bv = L0_attn_b_qkv[2 * D:3 * D]
    wqT = np.ascontiguousarray(wq.T)
    wkT = np.ascontiguousarray(wk.T)
    wvT = np.ascontiguousarray(wv.T)
    ball = np.ascontiguousarray(
        np.stack([bq.reshape(FC, 128).T, bk.reshape(FC, 128).T], axis=1))
    bvrow = np.ascontiguousarray(bv[None, :])
    in_maps = []
    for c in range(8):
        b, h = c // 2, c % 2
        xT = np.ascontiguousarray(L0_attn_x[b, h * T:(h + 1) * T, :].T)
        cos, sin = rope_tables(np.arange(h * T, (h + 1) * T))
        cos2 = np.ascontiguousarray(np.concatenate([cos, cos], axis=0))
        sin2 = np.ascontiguousarray(np.concatenate([sin, sin], axis=0))
        in_maps.append(dict(xT=xT, wqT=wqT, wkT=wkT, wvT=wvT, ball_i=ball, bvrow_i=bvrow,
                            cosq_i=cos2, sinq_i=sin2))
    return in_maps


def kernel(**inputs):
    raise NotImplementedError("wip")



def build_attn0():
    nc = bass.Bass("TRN2", target_bir_lowering=False)
    d = NS()
    d.xT = nc.dram_tensor("xT", [D, T], F32, kind="ExternalInput").ap()
    d.qr = nc.dram_tensor("qr_i", [D, T], BF16, kind="ExternalInput").ap()
    d.kown = nc.dram_tensor("kown_i", [D, T], BF16, kind="ExternalInput").ap()
    d.kpair = nc.dram_tensor("kpair_i", [D, 2 * T], BF16, kind="ExternalInput").ap()
    d.vown = nc.dram_tensor("vown_i", [T, D], BF16, kind="ExternalInput").ap()
    d.vpair = nc.dram_tensor("vpair_i", [2 * T, D], BF16, kind="ExternalInput").ap()
    d.woT = nc.dram_tensor("woT_i", [D, D], F32, kind="ExternalInput").ap()
    d.bout = nc.dram_tensor("bout_i", [128, FC], F32, kind="ExternalInput").ap()
    d.bvec = nc.dram_tensor("bvec_i", [128, 2], F32, kind="ExternalInput").ap()
    d.acc_o = nc.dram_tensor("acc_o", [D, T], F32, kind="ExternalOutput").ap()
    if DEBUG_ATTN:
        d.attn_o = nc.dram_tensor("attn_dbg", [D, T], BF16, kind="ExternalOutput").ap()
        d.rb_o = nc.dram_tensor("rb_dbg", [128, T], F32, kind="ExternalOutput").ap()
        d.ptall_o = nc.dram_tensor("ptall_dbg", [8 * 128, T], BF16, kind="ExternalOutput").ap()
        d.sums_o = nc.dram_tensor("sums_dbg", [1, T], F32, kind="ExternalOutput").ap()

    ctx = Ctx(nc)
    import contextlib
    with contextlib.ExitStack() as stack:
        acc = stack.enter_context(nc.sbuf_tensor("acc", [128, FC, T], F32))
        attn = stack.enter_context(nc.sbuf_tensor("attn", [128, FC, T], BF16))
        qh = stack.enter_context(nc.sbuf_tensor("qh", [128, 2, T], BF16))
        kh = stack.enter_context(nc.sbuf_tensor("kh", [128, 2, T], BF16))
        kph = stack.enter_context(nc.sbuf_tensor("kph", [128, 2, 2 * T], BF16))
        voh = stack.enter_context(nc.sbuf_tensor("voh", [128, 2, 8, 128], BF16))
        vph = stack.enter_context(nc.sbuf_tensor("vph", [128, 2, 16, 128], BF16))
        pt = stack.enter_context(nc.sbuf_tensor("pt", [128, 2, T], BF16))
        nsq = stack.enter_context(nc.sbuf_tensor("nsq", [128, 2 * T], BF16))
        rbsb = stack.enter_context(nc.sbuf_tensor("rbsb", [128, T], F32))
        rrow = stack.enter_context(nc.sbuf_tensor("rrow", [1, T], F32))
        rrowb = stack.enter_context(nc.sbuf_tensor("rrowb", [1, T], BF16))
        crowb = stack.enter_context(nc.sbuf_tensor("crowb", [1, 2], BF16))
        crow = stack.enter_context(nc.sbuf_tensor("crow", [1, 8], F32))
        biasb = stack.enter_context(nc.sbuf_tensor("biasb", [128, 4], F32))
        bvec = stack.enter_context(nc.sbuf_tensor("bvec", [128, 2], F32))
        bout = stack.enter_context(nc.sbuf_tensor("bout", [128, FC], F32))
        tri = stack.enter_context(nc.sbuf_tensor("tri", [128, 128], F32))
        wof = stack.enter_context(nc.sbuf_tensor("wof", [128, 2, FC, 128], F32))
        wob = stack.enter_context(nc.sbuf_tensor("wob", [128, 2, FC, 128], BF16))
        ones128 = stack.enter_context(nc.sbuf_tensor("ones128", [128, 1], F32))
        ones128b = stack.enter_context(nc.sbuf_tensor("ones128b", [128, 1], BF16))
        ones1r = stack.enter_context(nc.sbuf_tensor("ones1r", [1, 128], F32))
        ones1rb = stack.enter_context(nc.sbuf_tensor("ones1rb", [1, 128], BF16))
        ps = stack.enter_context(nc.psum_tensor("ps", [128, 8, 512], F32))
        block = stack.enter_context(nc.Block())
        stack.enter_context(open_sems(nc, ctx, ALL_SEMS))
        psf = ps.rearrange("p a b -> p (a b)")

        ctx.run("gpsimd", lambda e: e.memset(ones1rb[:, :], 1.0))
        ctx.run("gpsimd", lambda e: e.memset(ones128b[:, :], 1.0))
        ctx.run("gpsimd", lambda e: e.memset(ones128[:, :], 1.0))
        ctx.run("gpsimd", lambda e: e.memset(ones1r[:, :], 1.0))
        ctx.run("gpsimd", lambda e: e.memset(tri[:, :], 0.0))
        ctx.run("gpsimd", lambda e: e.affine_select(
            out=tri[:, :], in_=tri[:, :], compare_op=ALU.is_ge, fill=NEG,
            base=0, pattern=[[1, 128]], channel_multiplier=-1))
        g_const = ctx.c("g")

        for i in range(4):
            dma_in(ctx, "sync", acc[:, 4 * i:4 * (i + 1), :],
                   d.xT.rearrange("(c p) t -> p c t", p=128)[:, 4 * i:4 * (i + 1), :])
        dma_in(ctx, "sync", bout[:, :], d.bout)
        dma_in(ctx, "sync", bvec[:, :], d.bvec)
        ld0 = ctx.c("ld")

        qr_r = d.qr.rearrange("(c p) t -> p c t", p=128)
        kown_r = d.kown.rearrange("(c p) t -> p c t", p=128)
        kpair_r = d.kpair.rearrange("(c p) t -> p c t", p=128)
        vown_r = d.vown.rearrange("(c p) dd -> p c dd", p=128)
        vpair_r = d.vpair.rearrange("(c p) dd -> p c dd", p=128)

        hfree = {}
        ev_attn_prev = None
        sum_read_prev = None
        for h in range(NH):
            slot = h % 2
            dw = [hfree[slot]] if slot in hfree else []
            dma_in(ctx, "sync", qh[:, slot, :], qr_r[:, h, :], waits=dw, sem="ld")
            dma_in(ctx, "sync", kh[:, slot, :], kown_r[:, h, :], sem="ld")
            dma_in(ctx, "sync", kph[:, slot, :], kpair_r[:, h, :], sem="ld")
            dma_in(ctx, "sync", voh[:, slot, :, :],
                   vown_r[:, :, 128 * h:128 * (h + 1)], sem="ld")
            dma_in(ctx, "sync", vph[:, slot, :, :],
                   vpair_r[:, :, 128 * h:128 * (h + 1)], sem="ld")
            ldh = ctx.c("ld")
            lw = [("ld", ldh)]

            if FIXED_C is not None:
                cC = ctx.run("gpsimd", lambda e: e.memset(biasb[:, 0:1], -FIXED_C))
                b1 = cC
                b2 = ctx.run("vector", lambda e: e.tensor_add(
                    biasb[:, 1:2], biasb[:, 0:1], bvec[:, 0:1]),
                    waits=[cC, ("ld", ld0), ("v", ctx.c("v"))])
                b3 = ctx.run("vector", lambda e: e.tensor_add(
                    biasb[:, 2:3], biasb[:, 0:1], bvec[:, 1:2]), waits=[b2])
                last_read = b3
            if FIXED_C is None:
                sq1 = ctx.run("scalar", lambda e, slot=slot: e.activation(
                    nsq[:, 0:T], qh[:, slot, :], AF.Square),
                    waits=lw + [("s", ctx.c("s")), ("t", ctx.c("t"))])
                nmax = []
                for half in range(2):
                    w0 = [sq1]
                    if half == 0:
                        w0 += [("g", g_const)]
                        if sum_read_prev is not None:
                            w0.append(sum_read_prev)
                    ctx.emit("tensor", lambda e, h2=half: e.matmul(
                        ps[0:1, 6 + h2, :], ones128b[:, :], nsq[:, 512 * h2:512 * (h2 + 1)],
                        start=True, stop=True), waits=w0, inc=("t" if half == 1 else None))
                tq = ctx.c("t")
                n1 = ctx.run("vector", lambda e: e.reduce_max(
                    crow[:, 0:1], ps[0:1, 6, :], axis=AX.X), waits=[("t", tq), ("v", ctx.c("v"))])
                n2 = ctx.run("vector", lambda e: e.reduce_max(
                    crow[:, 1:2], ps[0:1, 7, :], axis=AX.X), waits=[n1])
                sq2 = ctx.run("scalar", lambda e, slot=slot: e.activation(
                    nsq[:, :], kph[:, slot, :], AF.Square), waits=lw + [sq1, ("t", tq)])
                last_read = n2
                for rnd in range(2):
                    for half in range(2):
                        w0 = [sq2] + ([last_read] if half == 0 else [])
                        ctx.emit("tensor", lambda e, rnd=rnd, h2=half: e.matmul(
                            ps[0:1, 6 + h2, :], ones128b[:, :],
                            nsq[:, 1024 * rnd + 512 * h2:1024 * rnd + 512 * (h2 + 1)],
                            start=True, stop=True), waits=w0, inc=("t" if half == 1 else None))
                    tk = ctx.c("t")
                    m1 = ctx.run("vector", lambda e, rnd=rnd: e.reduce_max(
                        crow[:, 2 + 2 * rnd:3 + 2 * rnd], ps[0:1, 6, :], axis=AX.X),
                        waits=[("t", tk), ("v", ctx.c("v"))])
                    m2 = ctx.run("vector", lambda e, rnd=rnd: e.reduce_max(
                        crow[:, 3 + 2 * rnd:4 + 2 * rnd], ps[0:1, 7, :], axis=AX.X),
                        waits=[m1])
                    last_read = m2
                c1 = ctx.run("vector", lambda e: e.tensor_max(
                    crow[:, 0:1], crow[:, 0:1], crow[:, 1:2]), waits=[last_read])
                c2 = ctx.run("vector", lambda e: e.tensor_max(
                    crow[:, 2:3], crow[:, 2:3], crow[:, 3:4]), waits=[c1])
                c3 = ctx.run("vector", lambda e: e.tensor_max(
                    crow[:, 2:3], crow[:, 2:3], crow[:, 4:5]), waits=[c2])
                c4 = ctx.run("vector", lambda e: e.tensor_max(
                    crow[:, 2:3], crow[:, 2:3], crow[:, 5:6]), waits=[c3])
                c5 = ctx.run("vector", lambda e: e.tensor_add(
                    crow[:, 6:7], crow[:, 0:1], crow[:, 2:3]), waits=[c4])
                c6 = ctx.run("vector", lambda e: e.tensor_scalar(
                    crow[:, 6:7], crow[:, 6:7], -0.5 * SCALE0, -1.0, ALU.mult, ALU.add),
                    waits=[c5])
                c7 = ctx.run("vector", lambda e: e.tensor_copy(crowb[:, 0:1], crow[:, 6:7]),
                             waits=[c6])
                tb_w = [c7]
                if ev_attn_prev is not None:
                    tb_w.append(ev_attn_prev)
                tb = ctx.emit("tensor", lambda e: e.matmul(
                    ps[:, 2, 0:1], ones1rb[:, :], crowb[:, 0:1], start=True, stop=True),
                    waits=tb_w, inc="t")
                b1 = ctx.run("vector", lambda e: e.tensor_copy(biasb[:, 0:1], ps[:, 2, 0:1]),
                             waits=[("t", tb), ("v", ctx.c("v"))])
                b2 = ctx.run("vector", lambda e: e.tensor_add(
                    biasb[:, 1:2], biasb[:, 0:1], bvec[:, 0:1]), waits=[b1, ("ld", ld0)])
                b3 = ctx.run("vector", lambda e: e.tensor_add(
                    biasb[:, 2:3], biasb[:, 0:1], bvec[:, 1:2]), waits=[b2])

            s_evac = {}
            pt_read = {}
            last_contrib = {}
            for blk in range(NBLK):
                own = blk < 8
                kb = blk if own else blk - 8
                q0 = kb * 128 if own else 0
                for ci, c0 in enumerate(range(q0 // 512 * 512, T, 512)):
                    last_contrib[c0 // 512] = (blk, ci)
            for blk in range(NBLK):
                own = blk < 8
                kb = blk if own else blk - 8
                sbank = (blk % 2) * 2
                slot2 = blk % 2
                q0 = kb * 128 if own else 0
                kap = kh if own else kph
                bias_col = 0 if own else 1 + kb // 8

                w0 = list(lw)
                if sbank in s_evac:
                    w0.append(s_evac[sbank])
                if blk == 1:
                    w0.append(b1)
                chunks = list(range(q0 // 512 * 512, T, 512))
                for ci, c0 in enumerate(chunks):
                    lo, hi = max(q0, c0), c0 + 512
                    ctx.emit("tensor", lambda e, kb=kb, lo=lo, hi=hi, slot=slot,
                             sbank=sbank, kap=kap: e.matmul(
                        psf[:, sbank * 512 + lo:sbank * 512 + hi],
                        kap[:, slot, 128 * kb:128 * (kb + 1)], qh[:, slot, lo:hi],
                        start=True, stop=True),
                        waits=(w0 if ci == 0 else []),
                        inc=("t" if ci == len(chunks) - 1 else None))
                tS = ("t", ctx.c("t"))
                exp_wait = [tS, b3, ("s", ctx.c("s"))]
                if own and not SKIP_TRI:
                    tmk = ctx.run("vector", lambda e, q0=q0, sbank=sbank: e.tensor_add(
                        psf[:, sbank * 512 + q0:sbank * 512 + q0 + 128],
                        psf[:, sbank * 512 + q0:sbank * 512 + q0 + 128], tri[:, :]),
                        waits=[tS, ("v", ctx.c("v")), ("g", g_const)])
                    exp_wait.append(tmk)
                if slot2 in pt_read:
                    exp_wait.append(pt_read[slot2])
                sexp = ctx.run("scalar", lambda e, q0=q0, sbank=sbank, slot2=slot2,
                               bias_col=bias_col: e.activation(
                    pt[:, slot2, q0:T], psf[:, sbank * 512 + q0:sbank * 512 + T],
                    AF.Exp, bias=biasb[:, bias_col:bias_col + 1], scale=SCALE0),
                    waits=exp_wait)
                if DEBUG_ATTN and h == NH - 1 and blk < 8:
                    dma_out(ctx, "gpsimd", d.ptall_o[128 * blk:128 * (blk + 1), :],
                            pt[:, slot2, :], waits=[sexp])
                    pt_read[slot2] = ("st", ctx.c("st"))
                s_evac[sbank] = sexp
                for ci, c0 in enumerate(chunks):
                    lo, hi = max(q0, c0), c0 + 512
                    sw = [sexp]
                    if blk == 0 and ci == 0:
                        sw.append(last_read)
                    is_last = last_contrib[c0 // 512] == (blk, ci)
                    ctx.emit("tensor", lambda e, lo=lo, hi=hi, c0=c0, slot2=slot2,
                             is_last=is_last, blk=blk: e.matmul(
                        ps[0:1, 6 + c0 // 512, lo - c0:hi - c0], ones128b[:, :],
                        pt[:, slot2, lo:hi],
                        start=(blk == 0), stop=is_last),
                        waits=sw, inc=("t" if blk == NBLK - 1 and ci == len(chunks) - 1 else None))
                    av_w = []
                    if blk == 0 and ci == 0 and ev_attn_prev is not None:
                        av_w.append(ev_attn_prev)
                    vap = voh if own else vph
                    ctx.emit("tensor", lambda e, lo=lo, hi=hi, kb=kb, slot=slot,
                             slot2=slot2, vap=vap, is_last=is_last, blk=blk: e.matmul(
                        psf[:, 4 * 512 + lo:4 * 512 + hi],
                        vap[:, slot, kb, :], pt[:, slot2, lo:hi],
                        start=(blk == 0), stop=is_last),
                        waits=av_w, inc=("t" if blk == NBLK - 1 and ci == len(chunks) - 1 else None))
                pt_read[slot2] = ("t", ctx.c("t"))
            t_all = ctx.c("t")

            rv_extra = []
            if DEBUG_ATTN and h == NH - 1:
                dsum = ctx.run("vector", lambda e: e.tensor_copy(
                    rrow[:, :], ps[0:1, 6:8, :].rearrange("p a b -> p (a b)")),
                    waits=[("t", t_all), ("v", ctx.c("v"))])
                dma_out(ctx, "gpsimd", d.sums_o, rrow[:, :], waits=[dsum])
                rv_extra = [("st", ctx.c("st"))]
            rv = ctx.run("vector", lambda e: e.reciprocal(
                rrow[:, :], ps[0:1, 6:8, :].rearrange("p a b -> p (a b)")),
                waits=[("t", t_all), ("v", ctx.c("v"))] + rv_extra)
            rvb = ctx.run("vector", lambda e: e.tensor_copy(rrowb[:, :], rrow[:, :]),
                          waits=[rv])
            for half in range(2):
                ctx.emit("tensor", lambda e, h2=half: e.matmul(
                    ps[:, 6 + h2, :], ones1rb[:, :], rrowb[:, 512 * h2:512 * (h2 + 1)],
                    start=True, stop=True), waits=[rvb], inc=("t" if half == 1 else None))
            trb = ctx.c("t")
            vrb = ctx.run("vector", lambda e: e.tensor_copy(
                rbsb[:, :], ps[:, 6:8, :].rearrange("p a b -> p (a b)")),
                waits=[("t", trb), ("v", ctx.c("v"))])
            ev = ctx.run("vector", lambda e, h=h: e.tensor_mul(
                attn[:, h, :], ps[:, 4:6, :].rearrange("p a b -> p (a b)"), rbsb[:, :]),
                waits=[vrb])
            ev_attn_prev = ev
            sum_read_prev = vrb
            hfree[slot] = ("t", t_all)
            if DEBUG_ATTN:
                dma_out(ctx, "gpsimd",
                        d.attn_o.rearrange("(c p) t -> p c t", p=128)[:, h, :],
                        attn[:, h, :], waits=[ev])
                if h == NH - 1:
                    dma_out(ctx, "gpsimd", d.rb_o, rbsb[:, :], waits=[ev])

        womarks = {}
        evo = {}
        for o in range(FC):
            wslot = o % 2
            dw = [womarks[wslot]] if wslot in womarks else []
            dma_in(ctx, "sync", wof[:, wslot, :, :],
                   d.woT.rearrange("(c p) oo -> p c oo", p=128)[:, :, 128 * o:128 * (o + 1)],
                   waits=dw, sem="wl")
            wl = ctx.c("wl")
            cst = ctx.run("vector", lambda e, wslot=wslot: e.tensor_copy(
                wob[:, wslot, :, :], wof[:, wslot, :, :]),
                waits=[("wl", wl), ("v", ctx.c("v")), ev_attn_prev])
            womarks[wslot] = cst
            bank = (o % 2) * 2
            w0 = [cst]
            if bank in evo:
                w0.append(evo[bank])
            for f in range(FC):
                for half in range(2):
                    ctx.emit("tensor", lambda e, f=f, h2=half, o=o, bank=bank,
                             wslot=wslot: e.matmul(
                        ps[:, bank + h2, :], wob[:, wslot, f, :],
                        attn[:, f, 512 * h2:512 * (h2 + 1)],
                        start=(f == 0), stop=(f == FC - 1)),
                        waits=(w0 if (f == 0 and half == 0) else []),
                        inc=("t" if (f == FC - 1 and half == 1) else None))
            to = ctx.c("t")
            eo = ctx.run("vector", lambda e, o=o, bank=bank: e.scalar_tensor_tensor(
                acc[:, o, :], ps[:, bank:bank + 2, :].rearrange("p a b -> p (a b)"),
                bout[:, o:o + 1], acc[:, o, :], ALU.add, ALU.add),
                waits=[("t", to), ("v", ctx.c("v")), ("ld", ld0)])
            evo[bank] = eo
            dma_out(ctx, "gpsimd", d.acc_o.rearrange("(c p) t -> p c t", p=128)[:, o, :],
                    acc[:, o, :], waits=[eo])

        emit_final_wait(ctx)
        ctx.finish(block)
    return nc


def prep_attn0_inputs(L0_attn_x, L0_attn_w_out, L0_attn_b_out, aouts):
    woT = np.ascontiguousarray(L0_attn_w_out.T)
    boutm = np.ascontiguousarray(L0_attn_b_out.reshape(FC, 128).T)
    in_maps = []
    for c in range(8):
        b, h = c // 2, c % 2
        r0, r1 = 2 * b, 2 * b + 1
        kpair = np.ascontiguousarray(
            np.concatenate([aouts[r0]["kr_o"], aouts[r1]["kr_o"]], axis=1))
        vpair = np.ascontiguousarray(
            np.concatenate([aouts[r0]["vtm_o"], aouts[r1]["vtm_o"]], axis=0))
        bvec = np.zeros((128, 2), np.float32)
        bvec[:, h] = NEG
        bvec[:, 1 - h] = NEG if h == 0 else 0.0
        xT = np.ascontiguousarray(L0_attn_x[b, h * T:(h + 1) * T, :].T)
        in_maps.append(dict(
            xT=xT, qr_i=aouts[c]["qr_o"], kown_i=aouts[c]["kr_o"],
            kpair_i=kpair, vown_i=aouts[c]["vtm_o"], vpair_i=vpair,
            woT_i=woT, bout_i=boutm, bvec_i=bvec))
    return in_maps



MH = 5632
MC = MH // 128
DEBUG_FFN = False
NTH_DBG = 2
NGRP_DBG = 4

def build_ffn0():
    nc = bass.Bass("TRN2", target_bir_lowering=False)
    d = NS()
    d.xnbf = nc.dram_tensor("xnbf_i", [D, T], BF16, kind="ExternalInput").ap()
    d.xnlo = nc.dram_tensor("xnlo_i", [D, T], BF16, kind="ExternalInput").ap()
    d.acc = nc.dram_tensor("acc_i", [D, T], F32, kind="ExternalInput").ap()
    d.w12t = nc.dram_tensor("w12t_i", [2, MC, 128, FC, 128], F32, kind="ExternalInput").ap()
    d.w3t = nc.dram_tensor("w3t_i", [MC, 128, D], F32, kind="ExternalInput").ap()
    d.acc_o = nc.dram_tensor("acc_o", [D, T], F32, kind="ExternalOutput").ap()
    if DEBUG_FFN:
        d.hid_o = nc.dram_tensor("hid_dbg", [128, 11, 512], BF16, kind="ExternalOutput").ap()
        d.q3_o = nc.dram_tensor("q3_dbg", [128, 11, D], BF16, kind="ExternalOutput").ap()

    W = 512
    ctx = Ctx(nc)
    import contextlib
    with contextlib.ExitStack() as stack:
        acc = stack.enter_context(nc.sbuf_tensor("acc", [128, FC, W], F32))
        xnbf = stack.enter_context(nc.sbuf_tensor("xnbf", [128, FC, W], BF16))
        xnlo = stack.enter_context(nc.sbuf_tensor("xnlo", [128, FC, W], BF16))
        gw = stack.enter_context(nc.sbuf_tensor("gw", [128, 2, FC * 128], F32))
        gq = stack.enter_context(nc.sbuf_tensor("gq", [128, 2, FC * 128], BF16))
        g3w = stack.enter_context(nc.sbuf_tensor("g3w", [128, 1, D], F32))
        g3q = stack.enter_context(nc.sbuf_tensor("g3q", [128, 11, D], BF16))
        hid = stack.enter_context(nc.sbuf_tensor("hid", [128, 11, W], BF16))
        gact = stack.enter_context(nc.sbuf_tensor("gact", [128, W], F32))
        masks = stack.enter_context(nc.sbuf_tensor("masks", [128, FC * 128], F32))
        gsum = stack.enter_context(nc.sbuf_tensor("gsum", [128, 4], F32))
        grow = stack.enter_context(nc.sbuf_tensor("grow", [1, 8], F32))
        tbc = stack.enter_context(nc.sbuf_tensor("tbc", [128, 8], F32))
        ones128 = stack.enter_context(nc.sbuf_tensor("ones128", [128, 1], F32))
        ones1r = stack.enter_context(nc.sbuf_tensor("ones1r", [1, 128], F32))
        ps = stack.enter_context(nc.psum_tensor("ps", [128, 8, 512], F32))
        block = stack.enter_context(nc.Block())
        stack.enter_context(open_sems(nc, ctx, ALL_SEMS))

        ctx.run("gpsimd", lambda e: e.memset(ones128[:, :], 1.0))
        ctx.run("gpsimd", lambda e: e.memset(ones1r[:, :], 1.0))
        g_const = ctx.c("g")

        w12f = d.w12t.rearrange("a b p c m -> (a b) p (c m)")
        srcs = [(w12f, ci, 0) for ci in range(MC)] + \
               [(w12f, MC + ci, 1) for ci in range(MC)] + \
               [(d.w3t, ci, 2) for ci in range(MC)]
        gm = {}
        first_col = {0: True, 1: True, 2: True}
        last_gsum = None
        for n, (w, idx, col) in enumerate(srcs):
            slot = n % 2
            wsl = gw[:, slot, :]
            dw = [gm[slot]] if slot in gm else []
            dma_in(ctx, "sync", wsl, w[idx, :, :], waits=dw, sem="wl")
            wl = ctx.c("wl")
            aw = [("wl", wl), ("s", ctx.c("s"))]
            if last_gsum is not None:
                aw.append(last_gsum)
            sv = ctx.run("scalar", lambda e, wsl=wsl: e.activation(
                wsl, wsl, AF.Abs, accum_out=gsum[:, 3:4]), waits=aw)
            gm[slot] = sv
            if first_col[col]:
                am = ctx.run("vector", lambda e, col=col: e.tensor_copy(
                    gsum[:, col:col + 1], gsum[:, 3:4]), waits=[sv, ("v", ctx.c("v"))])
                first_col[col] = False
            else:
                am = ctx.run("vector", lambda e, col=col: e.tensor_add(
                    gsum[:, col:col + 1], gsum[:, col:col + 1], gsum[:, 3:4]),
                    waits=[sv, ("v", ctx.c("v"))])
            last_gsum = am
        ctx.emit("tensor", lambda e: e.matmul(
            ps[0:1, 7, 0:4], ones128[:, :], gsum[:, 0:4], start=True, stop=True),
            waits=[last_gsum, ("g", g_const)], inc="t")
        tg = ctx.c("t")
        g1 = ctx.run("vector", lambda e: e.tensor_scalar(
            grow[:, 0:4], ps[0:1, 7, 0:4], 0.5 / (MH * D), 0.5e-8, ALU.mult, ALU.add),
            waits=[("t", tg), ("v", ctx.c("v"))])
        g3_ = ctx.run("vector", lambda e: e.tensor_scalar_mul(
            grow[:, 4:8], grow[:, 0:4], -1.0), waits=[g1])
        ctx.emit("tensor", lambda e: e.matmul(
            ps[:, 7, 0:8], ones1r[:, :], grow[:, 0:8], start=True, stop=True),
            waits=[g3_], inc="t")
        tg2 = ctx.c("t")
        gb = ctx.run("vector", lambda e: e.tensor_copy(tbc[:, :], ps[:, 7, 0:8]),
                     waits=[("t", tg2), ("v", ctx.c("v"))])

        def quantize(dst_ap, src_ap, col, waits):
            m1 = ctx.run("vector", lambda e: e.tensor_scalar(
                masks[:, 0:src_ap.shape[-1]], src_ap, tbc[:, col:col + 1], None, ALU.is_ge),
                waits=list(waits) + [gb, ("v", ctx.c("v")), ("t", ctx.c("t"))])
            m2 = ctx.run("vector", lambda e: e.scalar_tensor_tensor(
                masks[:, 0:src_ap.shape[-1]], src_ap, tbc[:, 4 + col:5 + col],
                masks[:, 0:src_ap.shape[-1]], ALU.is_le, ALU.subtract), waits=[m1])
            q = ctx.run("vector", lambda e: e.tensor_scalar_mul(
                dst_ap, masks[:, 0:src_ap.shape[-1]], -1.0), waits=[m2])
            return q

        q_marks_init = last_gsum
        accr = d.acc.rearrange("(c p) t -> p c t", p=128)
        acco = d.acc_o.rearrange("(c p) t -> p c t", p=128)
        xnr = d.xnbf.rearrange("(c p) t -> p c t", p=128)
        xlr = d.xnlo.rearrange("(c p) t -> p c t", p=128)
        acc_rd = None
        for th in range(NTH_DBG):
            t0 = th * W
            ld_base = ctx.c("ld")
            for i in range(4):
                dma_in(ctx, "gpsimd", acc[:, 4 * i:4 * (i + 1), :],
                       accr[:, 4 * i:4 * (i + 1), t0:t0 + W],
                       waits=[acc_rd] if (acc_rd and i == 0) else [])
            dma_in(ctx, "gpsimd", xnbf[:, :, :], xnr[:, :, t0:t0 + W])
            dma_in(ctx, "gpsimd", xnlo[:, :, :], xlr[:, :, t0:t0 + W])
            ld0 = ctx.c("ld")

            q_marks = {"gw": q_marks_init} if th == 0 else q_marks
            g3_war = ctx.c("t")
            hid_read = [None] * 11
            dacc_mark = {}
            for grp in range(NGRP_DBG):
                q3_done = []
                for gi in range(11):
                    mi = grp * 11 + gi
                    dw = [q_marks["gw"]] if "gw" in q_marks else []
                    dma_in(ctx, "sync", gw[:, 0, :],
                           d.w12t[0, mi, :, :, :].rearrange("p c m -> p (c m)"),
                           waits=dw, sem="wl")
                    dma_in(ctx, "sync", gw[:, 1, :],
                           d.w12t[1, mi, :, :, :].rearrange("p c m -> p (c m)"),
                           waits=dw, sem="wl")
                    dw3 = [("t", g3_war)]
                    if q3_done:
                        dw3.append(q3_done[-1])
                    dma_in(ctx, "sync", g3w[:, 0, :], d.w3t[mi, :, :], waits=dw3, sem="wl")
                    wl = ctx.c("wl")
                    qq = []
                    for wsel in range(2):
                        q = quantize(gq[:, wsel, :], gw[:, wsel, :], wsel, [("wl", wl)])
                        qq.append(q)
                    q_marks["gw"] = qq[1]
                    q3 = quantize(g3q[:, gi, :], g3w[:, 0, :], 2, [("wl", wl)])
                    q3_done.append(q3)

                    for wsel in range(2):
                        bank = wsel
                        nmm = 0
                        for src_x in (xnbf, xnlo):
                            for f in range(FC):
                                w0 = []
                                if nmm == 0:
                                    w0 = [qq[wsel], ("ld", ld0)]
                                    if ("x", bank) in dacc_mark:
                                        w0.append(dacc_mark[("x", bank)])
                                ctx.emit("tensor", lambda e, f=f, bank=bank, wsel=wsel,
                                         src_x=src_x, nmm=nmm: e.matmul(
                                    ps[:, bank, :], gq[:, wsel, 128 * f:128 * (f + 1)],
                                    src_x[:, f, :],
                                    start=(nmm == 0), stop=(nmm == 31)),
                                    waits=w0,
                                    inc=("t" if nmm == 31 else None))
                                nmm += 1
                    tx = ctx.c("t")
                    sgw = [("t", tx), ("s", ctx.c("s"))]
                    if "gact" in dacc_mark:
                        sgw.append(dacc_mark["gact"])
                    sg = ctx.run("scalar", lambda e: e.activation(
                        gact[:, :], ps[:, 0, :], AF.Sigmoid), waits=sgw)
                    dacc_mark[("x", 0)] = sg
                    hw = [sg, ("v", ctx.c("v"))]
                    if hid_read[gi] is not None:
                        hw.append(hid_read[gi])
                    hm = ctx.run("vector", lambda e, gi=gi: e.tensor_mul(
                        hid[:, gi, :], ps[:, 1, :], gact[:, :]), waits=hw)
                    dacc_mark[("x", 1)] = hm
                    dacc_mark["gact"] = hm

                hready = ctx.c("v")
                for o in range(FC):
                    bank = 2 + o % 2
                    for gi in range(11):
                        w0 = []
                        if gi == 0:
                            w0 = [("v", hready)]
                            if bank in dacc_mark:
                                w0.append(dacc_mark[bank])
                        ctx.emit("tensor", lambda e, gi=gi, bank=bank, o=o: e.matmul(
                            ps[:, bank, :], g3q[:, gi, 128 * o:128 * (o + 1)],
                            hid[:, gi, :],
                            start=(gi == 0), stop=(gi == 10)),
                            waits=w0, inc=("t" if gi == 10 else None))
                    to = ctx.c("t")
                    ev = ctx.run("vector", lambda e, o=o, bank=bank: e.tensor_add(
                        acc[:, o, :], ps[:, bank, :], acc[:, o, :]),
                        waits=[("t", to), ("v", ctx.c("v")), ("ld", ld0)])
                    dacc_mark[bank] = ev
                tend = ("t", ctx.c("t"))
                if DEBUG_FFN and th == 0 and grp == 0:
                    dma_out(ctx, "gpsimd", d.hid_o, hid[:, :, :], waits=[tend])
                    dma_out(ctx, "gpsimd", d.q3_o, g3q[:, :, :], waits=[tend])
                    hid_read = [("st", ctx.c("st"))] * 11
                    g3_war = None
                for gi in range(11):
                    if hid_read[gi] is None or hid_read[gi][0] != "st":
                        hid_read[gi] = tend
                if g3_war is None:
                    g3_war = ctx.c("st")
                    g3_war = tend[1]
                else:
                    g3_war = tend[1]

            for o in range(FC):
                dma_out(ctx, "gpsimd", acco[:, o, t0:t0 + W],
                        acc[:, o, :], waits=[dacc_mark[2 + o % 2]])
            acc_rd = ("st", ctx.c("st"))

        emit_final_wait(ctx)
        ctx.finish(block)
    return nc


def prep_ffn0_inputs(w1, w2, w3, aouts, bouts):
    def tile_cols(w):
        wT = np.ascontiguousarray(w.T)
        t = wT.reshape(FC, 128, MC, 128)
        return np.ascontiguousarray(t.transpose(2, 1, 0, 3))
    w12t = np.ascontiguousarray(np.stack([tile_cols(w1), tile_cols(w2)]))
    w3T = np.ascontiguousarray(w3.T)
    w3t = np.ascontiguousarray(w3T.reshape(MC, 128, D))
    in_maps = []
    for c in range(8):
        in_maps.append(dict(xnbf_i=aouts[c]["xnbf_o"], xnlo_i=aouts[c]["xnlo_o"],
                            acc_i=bouts[c]["acc_o"], w12t_i=w12t, w3t_i=w3t))
    return in_maps



def build_l1pre():
    nc = bass.Bass("TRN2", target_bir_lowering=False)
    d = NS()
    d.acc = nc.dram_tensor("acc_i", [D, T], F32, kind="ExternalInput").ap()
    d.wqkT = nc.dram_tensor("wqkT_i", [D, 128], F32, kind="ExternalInput").ap()
    d.wvT = nc.dram_tensor("wvT_i", [D, D], F32, kind="ExternalInput").ap()
    d.xnbf_o = nc.dram_tensor("xnbf_o", [D, T], BF16, kind="ExternalOutput").ap()
    d.qk_o = nc.dram_tensor("qk_o", [128, T], BF16, kind="ExternalOutput").ap()
    d.vtm_o = nc.dram_tensor("vtm_o", [T, D], BF16, kind="ExternalOutput").ap()

    ctx = Ctx(nc)
    import contextlib
    with contextlib.ExitStack() as stack:
        xnbf = stack.enter_context(nc.sbuf_tensor("xnbf", [128, FC, T], BF16))
        wf = stack.enter_context(nc.sbuf_tensor("wf", [128, 1, D], F32))
        wb = stack.enter_context(nc.sbuf_tensor("wb", [128, FC, D], BF16))
        wqkb = stack.enter_context(nc.sbuf_tensor("wqkb", [128, FC, 128], BF16))
        sq = stack.enter_context(nc.sbuf_tensor("sq", [128, T], F32))
        stat = stack.enter_context(nc.sbuf_tensor("stat", [1, 5, T], F32))
        qk = stack.enter_context(nc.sbuf_tensor("qk", [128, T], BF16))
        vtm = stack.enter_context(nc.sbuf_tensor("vtm", [128, 1, D], BF16))
        ones128 = stack.enter_context(nc.sbuf_tensor("ones128", [128, 1], F32))
        ones1r = stack.enter_context(nc.sbuf_tensor("ones1r", [1, 128], F32))
        ps = stack.enter_context(nc.psum_tensor("ps", [128, 8, 512], F32))
        block = stack.enter_context(nc.Block())
        x_guard = nc.sbuf_tensor("x", [128, FC, T], F32)
        x = x_guard.__enter__()
        stack.enter_context(open_sems(nc, ctx, ALL_SEMS))
        ns = NS(ps=ps, sq=sq, stat=stat, ones128=ones128, ones1r=ones1r, xnlo=None, sqf=None)

        ctx.run("gpsimd", lambda e: e.memset(ones128[:, :], 1.0))
        ctx.run("gpsimd", lambda e: e.memset(ones1r[:, :], 1.0))
        ns.g_const = ctx.c("g")

        for i in range(4):
            dma_in(ctx, "sync", x[:, 4 * i:4 * (i + 1), :],
                   d.acc.rearrange("(c p) t -> p c t", p=128)[:, 4 * i:4 * (i + 1), :])
        ld_all = ctx.c("ld")
        xw = [("ld", ld_all)]

        v_xn = emit_layernorm(ctx, ns, x, xnbf, xw)
        for i in range(FC):
            dma_out(ctx, "gpsimd", d.xnbf_o.rearrange("(c p) t -> p c t", p=128)[:, i, :],
                    xnbf[:, i, :], waits=[v_xn] if i == FC - 1 else
                    [("v", v_xn[1] - 2 * (FC - 1 - i))])

        marks = {}
        for f in range(FC):
            stream_cast(ctx, wf, 1, marks, f,
                        d.wqkT.rearrange("(c p) o -> p c o", p=128)[:, f, :],
                        wqkb[:, f, :], extra_cast_waits=[("t", ctx.c("t"))])
        qkcast = ctx.all_eng()
        for half in range(2):
            for f in range(FC):
                inc = "t" if (f == FC - 1 and half == 1) else None
                ctx.emit("tensor", lambda e, f=f, h2=half: e.matmul(
                    ps[:, h2, :], wqkb[:, f, :], xnbf[:, f, 512 * h2:512 * (h2 + 1)],
                    start=(f == 0), stop=(f == FC - 1)),
                    waits=(qkcast + [v_xn] if (f == 0 and half == 0) else []), inc=inc)
        tqk = ctx.c("t")
        evq = ctx.run("vector", lambda e: e.tensor_copy(
            qk[:, :], ps[:, 0:2, :].rearrange("p a b -> p (a b)")),
            waits=[("t", tqk), ("v", ctx.c("v"))])
        dma_out(ctx, "gpsimd", d.qk_o, qk[:, :], waits=[evq])

        t_before = ctx.c("t")
        for f in range(FC):
            stream_cast(ctx, wf, 1, marks, f,
                        d.wvT.rearrange("(c p) o -> p c o", p=128)[:, f, :],
                        wb[:, f, :], extra_cast_waits=[("t", t_before)])
        vcast = ctx.all_eng()
        evac_mark = {}
        vtm_free = {}
        for tt in range(8):
            vslot = 0
            for og in range(4):
                g = tt * 4 + og
                bank = 2 + g % 2
                w0 = list(vcast) + [v_xn] if g < 2 else []
                if bank in evac_mark:
                    w0 = w0 + [evac_mark[bank]]
                for f in range(FC):
                    inc = "t" if f == FC - 1 else None
                    ctx.emit("tensor", lambda e, f=f, tt=tt, og=og, bank=bank: e.matmul(
                        ps[:, bank, :], xnbf[:, f, 128 * tt:128 * (tt + 1)],
                        wb[:, f, 512 * og:512 * (og + 1)],
                        start=(f == 0), stop=(f == FC - 1)),
                        waits=(w0 if f == 0 else []), inc=inc)
                tv = ctx.c("t")
                ew = [("t", tv), ("s", ctx.c("s"))]
                if og == 0 and vslot in vtm_free:
                    ew.append(vtm_free[vslot])
                ev = ctx.run("scalar", lambda e, vslot=vslot, og=og, bank=bank: e.copy(
                    vtm[:, vslot, 512 * og:512 * (og + 1)], ps[:, bank, :]), waits=ew)
                evac_mark[bank] = ev
            dma_out(ctx, "gpsimd", d.vtm_o.rearrange("(c p) dd -> p c dd", p=128)[:, tt, :],
                    vtm[:, vslot, :], waits=[evac_mark[2], evac_mark[3]])
            vtm_free[vslot] = ("st", ctx.c("st"))

        emit_final_wait(ctx)
        ctx.finish(block)
        x_guard.__exit__(None, None, None)
    return nc


def prep_l1pre_inputs(Wq, Wk, Wv, couts):
    wqk = np.concatenate([Wq, Wk], axis=0)
    wqkT = np.ascontiguousarray(wqk.T)
    wvT = np.ascontiguousarray(Wv.T)
    in_maps = []
    for c in range(8):
        in_maps.append(dict(acc_i=couts[c]["acc_o"], wqkT_i=wqkT, wvT_i=wvT))
    return in_maps



def build_attn1():
    nc = bass.Bass("TRN2", target_bir_lowering=False)
    d = NS()
    d.acc = nc.dram_tensor("acc_i", [D, T], F32, kind="ExternalInput").ap()
    d.qf = nc.dram_tensor("qf_i", [64, T], BF16, kind="ExternalInput").ap()
    d.kfp = nc.dram_tensor("kfp_i", [64, 2 * T], BF16, kind="ExternalInput").ap()
    d.v1p = nc.dram_tensor("v1p_i", [2 * T, D], BF16, kind="ExternalInput").ap()
    d.woT = nc.dram_tensor("woT_i", [D, D], F32, kind="ExternalInput").ap()
    d.acc_o = nc.dram_tensor("acc_o", [D, T], F32, kind="ExternalOutput").ap()

    ctx = Ctx(nc)
    import contextlib
    with contextlib.ExitStack() as stack:
        acc = stack.enter_context(nc.sbuf_tensor("acc", [128, FC, T], F32))
        v1 = stack.enter_context(nc.sbuf_tensor("v1", [128, 16, D], BF16))
        attn = stack.enter_context(nc.sbuf_tensor("attn", [128, FC, T], BF16))
        qf = stack.enter_context(nc.sbuf_tensor("qf", [64, T], BF16))
        kfp = stack.enter_context(nc.sbuf_tensor("kfp", [64, 2 * T], BF16))
        pb = stack.enter_context(nc.sbuf_tensor("pb", [128, 2 * T], BF16))
        ptb = stack.enter_context(nc.sbuf_tensor("ptb", [128, 16, 128], BF16))
        ident = stack.enter_context(nc.sbuf_tensor("ident", [128, 128], BF16))
        mx = stack.enter_context(nc.sbuf_tensor("mx", [128, 2], F32))
        wof = stack.enter_context(nc.sbuf_tensor("wof", [128, 2, FC, 128], F32))
        wob = stack.enter_context(nc.sbuf_tensor("wob", [128, 2, FC, 128], BF16))
        ps = stack.enter_context(nc.psum_tensor("ps", [128, 6, 512], F32))
        pstr = stack.enter_context(nc.psum_tensor("pstr", [128, 2, 1024], BF16))
        block = stack.enter_context(nc.Block())
        stack.enter_context(open_sems(nc, ctx, ALL_SEMS))
        psf = ps.rearrange("p a b -> p (a b)")

        ctx.run("gpsimd", lambda e: e.memset(ident[:, :], 0.0))
        idm = ctx.run("gpsimd", lambda e: e.affine_select(
            out=ident[:, :], in_=ident[:, :], compare_op=ALU.not_equal,
            fill=1.0, base=0, pattern=[[-1, 128]], channel_multiplier=1))

        for i in range(4):
            dma_in(ctx, "sync", acc[:, 4 * i:4 * (i + 1), :],
                   d.acc.rearrange("(c p) t -> p c t", p=128)[:, 4 * i:4 * (i + 1), :])
        dma_in(ctx, "sync", qf[:, :], d.qf)
        dma_in(ctx, "sync", kfp[:, :], d.kfp)
        for i in range(2):
            dma_in(ctx, "sync", v1[:, 8 * i:8 * (i + 1), :],
                   d.v1p.rearrange("(c p) dd -> p c dd", p=128)[:, 8 * i:8 * (i + 1), :])
        ld0 = ctx.c("ld")

        ev_av = {}
        pt_marks = {}
        pb_mark = None
        for qt in range(8):
            for n4 in range(4):
                w0 = [("ld", ld0)] if n4 == 0 else []
                if n4 == 0 and pb_mark is not None:
                    w0.append(pb_mark)
                ctx.emit("tensor", lambda e, qt=qt, n4=n4: e.matmul(
                    ps[:, n4, :], qf[:, 128 * qt:128 * (qt + 1)],
                    kfp[:, 512 * n4:512 * (n4 + 1)],
                    start=True, stop=True),
                    waits=w0, inc=("t" if n4 == 3 else None))
            tS = ctx.c("t")
            rm = ctx.run("vector", lambda e: e.reduce_max(
                mx[:, 0:1], psf[:, 0:2 * T], axis=AX.X, negate=True),
                waits=[("t", tS), ("v", ctx.c("v"))])
            sexp = ctx.run("scalar", lambda e: e.activation(
                pb[:, :], psf[:, 0:2 * T], AF.Exp, bias=mx[:, 0:1],
                accum_out=mx[:, 1:2]), waits=[rm, ("s", ctx.c("s"))])
            pb_mark = sexp
            rv = ctx.run("vector", lambda e: e.reciprocal(mx[:, 1:2], mx[:, 1:2]),
                         waits=[sexp])
            pn = ctx.run("vector", lambda e: e.tensor_scalar_mul(
                pb[:, :], pb[:, :], mx[:, 1:2]), waits=[rv])
            pb_mark = pn
            for kb in range(16):
                slot = kb % 2
                tw = [pn]
                if slot in pt_marks:
                    tw.append(pt_marks[slot])
                if qt == 0 and kb == 0:
                    tw.append(idm)
                ctx.emit("tensor", lambda e, kb=kb, slot=slot: e.transpose(
                    pstr[:, slot, 0:128],
                    pb[:, 128 * kb:128 * (kb + 1)], ident[:, :]),
                    waits=tw, inc="t")
                ttr = ctx.c("t")
                cpm = ctx.run("vector", lambda e, kb=kb, slot=slot: e.tensor_copy(
                    ptb[:, kb, :], pstr[:, slot, 0:128]),
                    waits=[("t", ttr), ("v", ctx.c("v"))])
                pt_marks[slot] = cpm
            vready = ctx.c("v")
            for dc in range(FC):
                bank = 4 + dc % 2
                w0 = [("v", vready)]
                if ("av", bank) in ev_av:
                    w0.append(ev_av[("av", bank)])
                for kb in range(16):
                    inc = "t" if kb == 15 else None
                    ctx.emit("tensor", lambda e, dc=dc, kb=kb, bank=bank: e.matmul(
                        ps[:, bank, 0:128], v1[:, kb, 128 * dc:128 * (dc + 1)],
                        ptb[:, kb, :],
                        start=(kb == 0), stop=(kb == 15)),
                        waits=(w0 if kb == 0 else []), inc=inc)
                tav = ctx.c("t")
                eav = ctx.run("vector", lambda e, dc=dc, qt=qt, bank=bank: e.tensor_copy(
                    attn[:, dc, 128 * qt:128 * (qt + 1)], ps[:, bank, 0:128]),
                    waits=[("t", tav), ("v", ctx.c("v"))])
                ev_av[("av", bank)] = eav
        attn_done = ctx.c("v")

        womarks = {}
        evo = {}
        for o in range(FC):
            wslot = o % 2
            dw = [womarks[wslot]] if wslot in womarks else []
            dma_in(ctx, "sync", wof[:, wslot, :, :],
                   d.woT.rearrange("(c p) oo -> p c oo", p=128)[:, :, 128 * o:128 * (o + 1)],
                   waits=dw, sem="wl")
            wl = ctx.c("wl")
            cst = ctx.run("vector", lambda e, wslot=wslot: e.tensor_copy(
                wob[:, wslot, :, :], wof[:, wslot, :, :]),
                waits=[("wl", wl), ("v", ctx.c("v")), ("v", attn_done)])
            womarks[wslot] = cst
            bank = (o % 2) * 2
            w0 = [cst]
            if bank in evo:
                w0.append(evo[bank])
            for f in range(FC):
                for half in range(2):
                    ctx.emit("tensor", lambda e, f=f, h2=half, o=o, bank=bank,
                             wslot=wslot: e.matmul(
                        ps[:, bank + h2, :], wob[:, wslot, f, :],
                        attn[:, f, 512 * h2:512 * (h2 + 1)],
                        start=(f == 0), stop=(f == FC - 1)),
                        waits=(w0 if (f == 0 and half == 0) else []),
                        inc=("t" if (f == FC - 1 and half == 1) else None))
            to = ctx.c("t")
            eo = ctx.run("vector", lambda e, o=o, bank=bank: e.tensor_add(
                acc[:, o, :], ps[:, bank:bank + 2, :].rearrange("p a b -> p (a b)"),
                acc[:, o, :]),
                waits=[("t", to), ("v", ctx.c("v")), ("ld", ld0)])
            evo[bank] = eo
            dma_out(ctx, "gpsimd", d.acc_o.rearrange("(c p) t -> p c t", p=128)[:, o, :],
                    acc[:, o, :], waits=[eo])

        emit_final_wait(ctx)
        ctx.finish(block)
    return nc


def prep_attn1_inputs(Wo, douts, eouts_acc):
    woT = np.ascontiguousarray(Wo.T)
    in_maps = []
    for c in range(8):
        b, h = c // 2, c % 2
        r0, r1 = 2 * b, 2 * b + 1
        kfp = np.ascontiguousarray(
            np.concatenate([douts[r0]["qk_o"][64:128], douts[r1]["qk_o"][64:128]], axis=1))
        v1p = np.ascontiguousarray(
            np.concatenate([douts[r0]["vtm_o"], douts[r1]["vtm_o"]], axis=0))
        in_maps.append(dict(acc_i=eouts_acc[c]["acc_o"], qf_i=douts[c]["qk_o"][0:64],
                            kfp_i=kfp, v1p_i=v1p, woT_i=woT))
    return in_maps



ME = 16
MD = 1408
MDC = MD // 128

def build_moe():
    nc = bass.Bass("TRN2", target_bir_lowering=False)
    d = NS()
    d.xnbf = nc.dram_tensor("xnbf_i", [D, T], BF16, kind="ExternalInput").ap()
    d.acc = nc.dram_tensor("acc_i", [D, T], F32, kind="ExternalInput").ap()
    d.gwT = nc.dram_tensor("gwT_i", [D, ME], F32, kind="ExternalInput").ap()
    d.wgt = nc.dram_tensor("wgt_i", [ME, MDC, 128, FC, 128], F32, kind="ExternalInput").ap()
    d.wut = nc.dram_tensor("wut_i", [ME, MDC, 128, FC, 128], F32, kind="ExternalInput").ap()
    d.wdt = nc.dram_tensor("wdt_i", [ME, MDC, 128, FC, 128], F32, kind="ExternalInput").ap()
    d.acc_o = nc.dram_tensor("acc_o", [D, T], F32, kind="ExternalOutput").ap()

    ctx = Ctx(nc)
    import contextlib
    with contextlib.ExitStack() as stack:
        acc = stack.enter_context(nc.sbuf_tensor("acc", [128, FC, T], F32))
        xnbf = stack.enter_context(nc.sbuf_tensor("xnbf", [128, FC, T], BF16))
        act = stack.enter_context(nc.sbuf_tensor("act", [128, MDC, T], BF16))
        wdgrp = stack.enter_context(nc.sbuf_tensor("wdgrp", [128, MDC, FC * 128], BF16))
        cwb = stack.enter_context(nc.sbuf_tensor("cwb", [128, T], BF16))
        wgb = stack.enter_context(nc.sbuf_tensor("wgb", [128, 2, FC * 128], BF16))
        wub = stack.enter_context(nc.sbuf_tensor("wub", [128, 2, FC * 128], BF16))
        gact = stack.enter_context(nc.sbuf_tensor("gact", [128, T], F32))
        actmp = stack.enter_context(nc.sbuf_tensor("actmp", [128, T], BF16))
        gwb = stack.enter_context(nc.sbuf_tensor("gwb", [128, FC, ME], BF16))
        lgr = stack.enter_context(nc.sbuf_tensor("lgr", [16, T], BF16))
        lgtm = stack.enter_context(nc.sbuf_tensor("lgtm", [128, 8, ME], BF16))
        sc = stack.enter_context(nc.sbuf_tensor("sc", [128, 8, ME], F32))
        wk = stack.enter_context(nc.sbuf_tensor("wk", [128, 2, ME], F32))
        cwtm = stack.enter_context(nc.sbuf_tensor("cwtm", [128, 8, ME], BF16))
        cwfm = stack.enter_context(nc.sbuf_tensor("cwfm", [16, T], BF16))
        esel = stack.enter_context(nc.sbuf_tensor("esel", [16, ME * 128], BF16))
        ident = stack.enter_context(nc.sbuf_tensor("ident", [128, 128], BF16))
        rmx = stack.enter_context(nc.sbuf_tensor("rmx", [128, 2], F32))
        ps = stack.enter_context(nc.psum_tensor("ps", [128, 6, 512], F32))
        pstr = stack.enter_context(nc.psum_tensor("pstr", [128, 2, 1024], BF16))
        block = stack.enter_context(nc.Block())
        stack.enter_context(open_sems(nc, ctx, ALL_SEMS))

        ctx.run("gpsimd", lambda e: e.memset(ident[:, :], 0.0))
        ctx.run("gpsimd", lambda e: e.affine_select(
            out=ident[:, :], in_=ident[:, :], compare_op=ALU.not_equal,
            fill=1.0, base=0, pattern=[[-1, 128]], channel_multiplier=1))
        ctx.run("gpsimd", lambda e: e.memset(esel[:, :], 1.0))
        ctx.run("gpsimd", lambda e: e.affine_select(
            out=esel[:, :], in_=esel[:, :], compare_op=ALU.is_ge,
            fill=0.0, base=0, pattern=[[1, ME * 128]], channel_multiplier=-128))
        em = ctx.run("gpsimd", lambda e: e.affine_select(
            out=esel[:, :], in_=esel[:, :], compare_op=ALU.is_ge,
            fill=0.0, base=127, pattern=[[-1, ME * 128]], channel_multiplier=128))

        for i in range(4):
            dma_in(ctx, "sync", acc[:, 4 * i:4 * (i + 1), :],
                   d.acc.rearrange("(c p) t -> p c t", p=128)[:, 4 * i:4 * (i + 1), :])
        for i in range(2):
            dma_in(ctx, "sync", xnbf[:, 8 * i:8 * (i + 1), :],
                   d.xnbf.rearrange("(c p) t -> p c t", p=128)[:, 8 * i:8 * (i + 1), :])
        dma_in(ctx, "gpsimd", gwb[:, :, :], d.gwT.rearrange("(c p) o -> p c o", p=128),
               sem="ld")
        ld0 = ctx.c("ld")

        for f in range(FC):
            for half in range(2):
                ctx.emit("tensor", lambda e, f=f, h2=half: e.matmul(
                    ps[0:16, h2, :], gwb[:, f, :], xnbf[:, f, 512 * h2:512 * (h2 + 1)],
                    start=(f == 0), stop=(f == FC - 1)),
                    waits=([("ld", ld0)] if (f == 0 and half == 0) else []),
                    inc=("t" if (f == FC - 1 and half == 1) else None))
        tlg = ctx.c("t")
        elg = ctx.run("vector", lambda e: e.tensor_copy(
            lgr[:, :], ps[0:16, 0:2, :].rearrange("p a b -> p (a b)")),
            waits=[("t", tlg), ("v", ctx.c("v"))])
        trm = {}
        for tt in range(8):
            slot = tt % 2
            tw = [elg, em] if tt == 0 else [elg]
            if slot in trm:
                tw.append(trm[slot])
            ctx.emit("tensor", lambda e, tt=tt, slot=slot: e.transpose(
                pstr[:, slot, 0:16], lgr[:, 128 * tt:128 * (tt + 1)], ident[0:16, 0:16]),
                waits=tw, inc="t")
            ttr = ctx.c("t")
            cp = ctx.run("vector", lambda e, tt=tt, slot=slot: e.tensor_copy(
                lgtm[:, tt, :], pstr[:, slot, 0:16]), waits=[("t", ttr), ("v", ctx.c("v"))])
            trm[slot] = cp
        cw_done = []
        for tt in range(8):
            r1 = ctx.run("vector", lambda e, tt=tt: e.reduce_max(
                rmx[:, 0:1], lgtm[:, tt, :], axis=AX.X, negate=True),
                waits=[trm[tt % 2], ("v", ctx.c("v"))])
            s1 = ctx.run("scalar", lambda e, tt=tt: e.activation(
                sc[:, tt, :], lgtm[:, tt, :], AF.Exp, bias=rmx[:, 0:1],
                accum_out=rmx[:, 1:2]), waits=[r1, ("s", ctx.c("s"))])
            r2 = ctx.run("vector", lambda e: e.reciprocal(rmx[:, 1:2], rmx[:, 1:2]),
                         waits=[s1])
            r3 = ctx.run("vector", lambda e, tt=tt: e.tensor_scalar_mul(
                sc[:, tt, :], sc[:, tt, :], rmx[:, 1:2]), waits=[r2])
            w = ctx.run("vector", lambda e, tt=tt: e.tensor_copy(wk[:, 0, :], sc[:, tt, :]),
                        waits=[r3])
            for it in range(6):
                m = ctx.run("vector", lambda e: e.reduce_max(rmx[:, 0:1], wk[:, 0, :],
                                                             axis=AX.X), waits=[w])
                eq = ctx.run("vector", lambda e: e.tensor_scalar(
                    wk[:, 1, :], wk[:, 0, :], rmx[:, 0:1], None, ALU.is_equal), waits=[m])
                t1 = ctx.run("vector", lambda e: e.scalar_tensor_tensor(
                    wk[:, 1, :], wk[:, 0, :], 1e30, wk[:, 1, :], ALU.add, ALU.mult),
                    waits=[eq])
                w = ctx.run("vector", lambda e: e.tensor_sub(wk[:, 0, :], wk[:, 0, :],
                                                             wk[:, 1, :]), waits=[t1])
            rl = ctx.run("vector", lambda e: e.tensor_relu(wk[:, 1, :], wk[:, 0, :]),
                         waits=[w])
            cwm = ctx.run("vector", lambda e, tt=tt: e.tensor_sub(
                cwtm[:, tt, :], sc[:, tt, :], wk[:, 1, :]), waits=[rl])
            cw_done.append(cwm)
        trm2 = {}
        for tt in range(8):
            slot = tt % 2
            tw = [cw_done[tt]]
            if slot in trm2:
                tw.append(trm2[slot])
            ctx.emit("tensor", lambda e, tt=tt, slot=slot: e.transpose(
                pstr[0:16, slot, 0:128], cwtm[:, tt, :], ident[:, :]),
                waits=tw, inc="t")
            ttr = ctx.c("t")
            cp = ctx.run("vector", lambda e, tt=tt, slot=slot: e.tensor_copy(
                cwfm[:, 128 * tt:128 * (tt + 1)], pstr[0:16, slot, 0:128]),
                waits=[("t", ttr), ("v", ctx.c("v"))])
            trm2[slot] = cp
        cwf_done = ("v", ctx.c("v"))

        wmarks = {}
        dacc = {}
        act_read = {}
        wd_read = None
        for e_ in range(ME):
            bw = [cwf_done]
            if ("x", 2) in dacc:
                bw.append(dacc[("x", 2)])
            if "cwb" in dacc:
                bw.append(dacc["cwb"])
            for half in range(2):
                ctx.emit("tensor", lambda e, e_=e_, h2=half: e.matmul(
                    ps[:, 2 + h2, :], esel[:, 128 * e_:128 * (e_ + 1)],
                    cwfm[:, 512 * h2:512 * (h2 + 1)],
                    start=True, stop=True),
                    waits=(bw if half == 0 else []), inc=("t" if half == 1 else None))
            tb = ctx.c("t")
            bm = ctx.run("scalar", lambda e: e.copy(
                cwb[:, :], ps[:, 2:4, :].rearrange("p a b -> p (a b)")),
                waits=[("t", tb), ("s", ctx.c("s"))])

            for mi in range(MDC):
                slot = mi % 2
                for nm, dr, buf in (("g", d.wgt, wgb), ("u", d.wut, wub)):
                    key = (nm, slot)
                    dw = [wmarks[key]] if key in wmarks else []
                    dma_in(ctx, "gpsimd", buf[:, slot, :],
                           dr[e_, mi, :, :, :].rearrange("p c m -> p (c m)"),
                           waits=dw, sem="wl")
                dwd = [wd_read] if wd_read is not None else []
                dma_in(ctx, "gpsimd", wdgrp[:, mi, :],
                       d.wdt[e_, mi, :, :, :].rearrange("p c m -> p (c m)"),
                       waits=dwd, sem="wl")
                wl = ctx.c("wl")
                for wsel, buf in ((0, wgb), (1, wub)):
                    bank = wsel * 2
                    for f in range(FC):
                        for half in range(2):
                            w0 = []
                            if f == 0 and half == 0:
                                w0 = [("wl", wl), ("ld", ld0)]
                                if wsel == 1:
                                    w0.append(bm)
                                if ("x", bank) in dacc:
                                    w0.append(dacc[("x", bank)])
                            ctx.emit("tensor", lambda e, f=f, h2=half, bank=bank,
                                     slot=slot, buf=buf: e.matmul(
                                ps[:, bank + h2, :], buf[:, slot, 128 * f:128 * (f + 1)],
                                xnbf[:, f, 512 * h2:512 * (h2 + 1)],
                                start=(f == 0), stop=(f == FC - 1)),
                                waits=w0,
                                inc=("t" if (f == FC - 1 and half == 1) else None))
                tx = ctx.c("t")
                wmarks[("g", slot)] = ("t", tx)
                wmarks[("u", slot)] = ("t", tx)
                sgw = [("t", tx), ("s", ctx.c("s"))]
                if "gact" in dacc:
                    sgw.append(dacc["gact"])
                sg = ctx.run("scalar", lambda e: e.activation(
                    gact[:, :], ps[:, 0:2, :].rearrange("p a b -> p (a b)"), AF.Silu),
                    waits=sgw)
                dacc[("x", 0)] = sg
                a1w = [sg, ("v", ctx.c("v"))]
                if "actmp" in dacc:
                    a1w.append(dacc["actmp"])
                a1 = ctx.run("vector", lambda e: e.tensor_mul(
                    actmp[:, :], ps[:, 2:4, :].rearrange("p a b -> p (a b)"), gact[:, :]),
                    waits=a1w)
                dacc[("x", 2)] = a1
                dacc["gact"] = a1
                aw = [a1, bm]
                if mi in act_read:
                    aw.append(act_read[mi])
                a2 = ctx.run("vector", lambda e, mi=mi: e.tensor_mul(
                    act[:, mi, :], actmp[:, :], cwb[:, :]), waits=aw)
                dacc["actmp"] = a2
                dacc["cwb"] = a2
            aready = ctx.c("v")

            for o in range(FC):
                pair = 4 if o % 2 == 0 else 0
                for mi in range(MDC):
                    for half in range(2):
                        w0 = []
                        if mi == 0 and half == 0:
                            w0 = [("v", aready)]
                            if ("dp", pair) in dacc:
                                w0.append(dacc[("dp", pair)])
                            if pair == 0 and ("x", 0) in dacc:
                                w0.append(dacc[("x", 0)])
                        ctx.emit("tensor", lambda e, mi=mi, h2=half, pair=pair,
                                 o=o: e.matmul(
                            ps[:, pair + h2, :], wdgrp[:, mi, 128 * o:128 * (o + 1)],
                            act[:, mi, 512 * h2:512 * (h2 + 1)],
                            start=(mi == 0), stop=(mi == MDC - 1)),
                            waits=w0,
                            inc=("t" if (mi == MDC - 1 and half == 1) else None))
                to = ctx.c("t")
                ev = ctx.run("vector", lambda e, o=o, pair=pair: e.tensor_add(
                    acc[:, o, :], ps[:, pair:pair + 2, :].rearrange("p a b -> p (a b)"),
                    acc[:, o, :]),
                    waits=[("t", to), ("v", ctx.c("v")), ("ld", ld0)])
                dacc[("dp", pair)] = ev
            tend = ("t", ctx.c("t"))
            wd_read = tend
            for mi in range(MDC):
                act_read[mi] = tend
            dacc[("x", 0)] = dacc[("dp", 0)] if ("dp", 0) in dacc else dacc[("x", 0)]
            dacc[("x", 2)] = dacc[("dp", 0)] if ("dp", 0) in dacc else dacc[("x", 2)]

        for o in range(FC):
            dma_out(ctx, "gpsimd", d.acc_o.rearrange("(c p) t -> p c t", p=128)[:, o, :],
                    acc[:, o, :], waits=[dacc[("dp", 4 if o % 2 == 0 else 0)]])
        emit_final_wait(ctx)
        ctx.finish(block)
    return nc


def prep_moe_inputs(gate_w, w_gate, w_up, w_down, douts, eouts):
    def tile_gu(w):
        out = np.empty((ME, MDC, 128, FC, 128), np.float32)
        for e in range(ME):
            wT = w[e].T
            t = wT.reshape(FC, 128, MDC, 128)
            out[e] = t.transpose(2, 1, 0, 3)
        return np.ascontiguousarray(out)
    def tile_d(w):
        out = np.empty((ME, MDC, 128, FC, 128), np.float32)
        for e in range(ME):
            wT = w[e].T
            t = wT.reshape(MDC, 128, FC, 128)
            out[e] = t
        return np.ascontiguousarray(out)
    wgt = tile_gu(w_gate)
    wut = tile_gu(w_up)
    wdt = tile_d(w_down)
    gwT = np.ascontiguousarray(gate_w.T)
    in_maps = []
    for c in range(8):
        in_maps.append(dict(xnbf_i=douts[c]["xnbf_o"], acc_i=eouts[c]["acc_o"],
                            gwT_i=gwT, wgt_i=wgt, wut_i=wut, wdt_i=wdt))
    return in_maps



_NC_CACHE = {}

def _get_nc(name, builder):
    if name not in _NC_CACHE:
        _NC_CACHE[name] = builder()
    return _NC_CACHE[name]


def kernel(L0_attn_x, L0_attn_w_qkv, L0_attn_b_qkv, L0_attn_w_out, L0_attn_b_out,
           L0_ffn_w1, L0_ffn_w2, L0_ffn_w3, L1_attn_Wq, L1_attn_Wk, L1_attn_Wv,
           L1_attn_Wo, L1_ffn_gate_w, L1_ffn_w_gate, L1_ffn_w_up, L1_ffn_w_down):
    cores = list(range(8))
    resA = run_bass_kernel_spmd(_get_nc("A", build_l0pre),
                                prep_l0pre_inputs(L0_attn_x, L0_attn_w_qkv, L0_attn_b_qkv),
                                core_ids=cores)
    resB = run_bass_kernel_spmd(_get_nc("B", build_attn0),
                                prep_attn0_inputs(L0_attn_x, L0_attn_w_out, L0_attn_b_out,
                                                  resA.results), core_ids=cores)
    resC = run_bass_kernel_spmd(_get_nc("C", build_ffn0),
                                prep_ffn0_inputs(L0_ffn_w1, L0_ffn_w2, L0_ffn_w3,
                                                 resA.results, resB.results), core_ids=cores)
    resD = run_bass_kernel_spmd(_get_nc("D", build_l1pre),
                                prep_l1pre_inputs(L1_attn_Wq, L1_attn_Wk, L1_attn_Wv,
                                                  resC.results), core_ids=cores)
    resE = run_bass_kernel_spmd(_get_nc("E", build_attn1),
                                prep_attn1_inputs(L1_attn_Wo, resD.results, resC.results),
                                core_ids=cores)
    resF = run_bass_kernel_spmd(_get_nc("F", build_moe),
                                prep_moe_inputs(L1_ffn_gate_w, L1_ffn_w_gate, L1_ffn_w_up,
                                                L1_ffn_w_down, resD.results, resE.results),
                                core_ids=cores)
    out = np.empty((4, 2048, 2048), np.float32)
    for c in range(8):
        b, h = c // 2, c % 2
        out[b, h * T:(h + 1) * T, :] = resF.results[c]["acc_o"].T
    return out
